# version 01 (final)
# baseline (speedup 1.0000x reference)
import contextlib
import math
import numpy as np
import ml_dtypes
import concourse.bass as bass
import concourse.mybir as mybir
from concourse.bass_utils import run_bass_kernel_spmd

F32 = mybir.dt.float32
BF16 = mybir.dt.bfloat16
AF = mybir.ActivationFunctionType
ALU = mybir.AluOpType
AX = mybir.AxisListType

RACE_DETECT = True

B, T, D = 16, 2048, 1024
NCORES = 8
BL = B // NCORES
NCH = D // 128
TB = 512
NTB = T // TB
EPS_LN = 1e-5


class Sem:
    __slots__ = ("h", "n")

    def __init__(self, h):
        self.h = h
        self.n = 0

    def inc(self, instr, k=1):
        instr.then_inc(self.h, k)
        self.n += k
        return instr


class Gate:
    __slots__ = ("sem", "cnt")

    def __init__(self):
        self.sem = None
        self.cnt = 0

    def wait(self, eng):
        if self.sem is not None:
            eng.wait_ge(self.sem.h, self.cnt)

    def set(self, sem):
        self.sem = sem
        self.cnt = sem.n


def vsync(b, instr):
    sem = b.sem("m_v")
    sem.inc(instr)
    b.nc.vector.wait_ge(sem.h, sem.n)
    return instr


def vchain(b, fn):
    sem = b.sem("m_v")
    instr = fn()
    sem.inc(instr)
    b.nc.vector.wait_ge(sem.h, sem.n)
    return instr


def vred(b, fn):
    sem = b.sem("m_v")
    sem.inc(b.nc.vector.engine_nop())
    b.nc.vector.wait_ge(sem.h, sem.n)
    instr = fn()
    sem.inc(instr)
    b.nc.vector.wait_ge(sem.h, sem.n)
    return instr


def phase_gate(b, engines):
    for e in engines:
        for sm in b.sems.values():
            if sm.n:
                e.wait_ge(sm.h, sm.n)


class Builder:
    def __init__(self, nc, ctx):
        self.nc = nc
        self.ctx = ctx
        self.gctx = ctx
        self.sems = {}
        self.dram = {}
        self._uid = 0

    def sem(self, name) -> Sem:
        if name not in self.sems:
            h = self.gctx.enter_context(self.nc.semaphore(name))
            self.sems[name] = Sem(h)
        return self.sems[name]

    def sb(self, name, shape, dtype):
        self._uid += 1
        return self.ctx.enter_context(
            self.nc.sbuf_tensor(f"sb{self._uid}_{name}", shape, dtype))

    def ps(self, name, shape, dtype=F32):
        self._uid += 1
        return self.ctx.enter_context(
            self.nc.psum_tensor(f"ps{self._uid}_{name}", shape, dtype))

    def din(self, name, shape, dtype=F32):
        ap = self.nc.dram_tensor(name, shape, dtype, kind="ExternalInput").ap()
        self.dram[name] = ap
        return ap

    def dout(self, name, shape, dtype=F32):
        ap = self.nc.dram_tensor(name, shape, dtype, kind="ExternalOutput").ap()
        self.dram[name] = ap
        return ap

    def dint(self, name, shape, dtype=F32):
        ap = self.nc.dram_tensor(name, shape, dtype, kind="Internal").ap()
        self.dram[name] = ap
        return ap

    def scope(self):
        import contextlib as _ct

        @_ct.contextmanager
        def _s():
            old = self.ctx
            with _ct.ExitStack() as es:
                self.ctx = es
                try:
                    yield
                finally:
                    self.ctx = old
        return _s()

    def init_fconsts(self, values, sem_v):
        t = self.sb("c_fconsts", [128, max(1, len(values))], F32)
        for i, v in enumerate(values):
            ii = self.nc.vector.memset(t[:, i:i + 1], float(v))
            self.nc.const_aps.register(F32, float(v), t[:, i:i + 1])
        if values:
            sem_v.inc(ii)


class Consts:
    def __init__(self, b: Builder):
        self.ones_bf = b.sb("c_ones_bf", [128, 1], BF16)
        self.ones1f = b.sb("c_ones1f", [1, 128], F32)
        self.ident_bf = b.sb("c_ident_bf", [128, 128], BF16)
        self.ident_f = b.sb("c_ident_f", [128, 128], F32)
        self.syncloads = [
            (self.ones1f[:], b.din("c_ones1f_in", [1, 128])),
            (self.ident_f[:], b.din("c_ident_f_in", [128, 128])),
        ]
        self.ones1b = b.sb("c_ones1b", [1, 128], BF16)
        self.castloads = [
            (self.ones_bf[:], b.din("c_ones_bf_in", [128, 1])),
            (self.ident_bf[:], b.din("c_ident_bf_in", [128, 128])),
            (self.ones1b[:], b.din("c_ones1b_in", [1, 128])),
        ]

    def emit_loads(self, b: Builder):
        s1 = b.sem("c_consts_sync")
        s2 = b.sem("c_consts_cast")
        for dst, src in self.syncloads:
            s1.inc(b.nc.sync.dma_start(dst, src), 16)
        for dst, src in self.castloads:
            s2.inc(b.nc.gpsimd.dma_start(dst, src), 16)
        return s1, s2


def host_consts(inp):
    return {
        "c_ones_bf_in": np.ones((128, 1), np.float32),
        "c_ones1b_in": np.ones((1, 128), np.float32),
        "c_ones1f_in": np.ones((1, 128), np.float32),
        "c_ident_bf_in": np.eye(128, dtype=np.float32),
        "c_ident_f_in": np.eye(128, dtype=np.float32),
    }


class LNState:
    def __init__(self, b: Builder, tag):
        self.xb = b.sb(f"{tag}_lnxb", [128, NCH, TB], BF16)
        self.sq = b.sb(f"{tag}_lnsq", [128, NCH, TB], BF16)
        self.sumr = b.sb(f"{tag}_lnsum", [1, 2 * TB], F32)
        self.rrow = b.sb(f"{tag}_lnr", [1, TB], F32)
        self.arow = b.sb(f"{tag}_lna", [1, TB], F32)
        self.trow = b.sb(f"{tag}_lnt", [1, TB], F32)
        self.stat_ps = b.ps(f"{tag}_lnstat", [1, 2, TB], F32)
        self.bc_ps = b.ps(f"{tag}_lnbc", [128, 2, TB], F32)


def emit_ln_stats(b, C, ln, xch, sem_t, sem_v, sem_s):
    nc = b.nc
    for c in range(NCH):
        iv = nc.vector.tensor_copy(ln.xb[:, c], xch[c])
        isq = nc.scalar.square(ln.sq[:, c], xch[c])
    sem_v.inc(iv)
    sem_s.inc(isq)
    nc.tensor.wait_ge(sem_v.h, sem_v.n)
    nc.tensor.wait_ge(sem_s.h, sem_s.n)
    for c in range(NCH):
        nc.tensor.matmul(ln.stat_ps[0:1, 0], C.ones_bf[:], ln.xb[:, c],
                         start=(c == 0), stop=(c == NCH - 1))
    for c in range(NCH):
        mm = nc.tensor.matmul(ln.stat_ps[0:1, 1], C.ones_bf[:], ln.sq[:, c],
                              start=(c == 0), stop=(c == NCH - 1))
    sem_t.inc(mm)
    nc.vector.wait_ge(sem_t.h, sem_t.n)
    nc.vector.tensor_single_scalar(ln.sumr[0:1, :],
                                   ln.stat_ps[0:1].rearrange("p a b -> p (a b)"),
                                   1.0 / D, ALU.mult)
    nc.vector.tensor_mul(ln.trow[:], ln.sumr[0:1, 0:TB], ln.sumr[0:1, 0:TB])
    sem_v.inc(nc.vector.tensor_sub(ln.trow[:], ln.sumr[0:1, TB:2 * TB], ln.trow[:]))
    nc.scalar.wait_ge(sem_v.h, sem_v.n)
    nc.scalar.activation(ln.rrow[:], ln.trow[:], AF.Ln, bias=EPS_LN)
    sem_s.inc(nc.scalar.activation(ln.rrow[:], ln.rrow[:], AF.Exp, scale=-0.5))
    nc.vector.wait_ge(sem_s.h, sem_s.n)
    nc.vector.tensor_mul(ln.arow[:], ln.sumr[0:1, 0:TB], ln.rrow[:])
    sem_v.inc(nc.vector.tensor_single_scalar(ln.arow[:], ln.arow[:], -1.0, ALU.mult))
    nc.tensor.wait_ge(sem_v.h, sem_v.n)
    nc.tensor.matmul(ln.bc_ps[:, 0], C.ones1f[:], ln.rrow[:], start=True, stop=True)
    sem_t.inc(nc.tensor.matmul(ln.bc_ps[:, 1], C.ones1f[:], ln.arow[:],
                               start=True, stop=True))


def emit_ln_normalize(b, ln, xch, dsts, tmp, sem_t, sem_v):
    nc = b.nc
    nc.vector.wait_ge(sem_t.h, sem_t.n)
    for c in range(NCH):
        nc.vector.tensor_mul(tmp[:], xch[c], ln.bc_ps[:, 0])
        i = nc.vector.tensor_add(dsts[c], tmp[:], ln.bc_ps[:, 1])
    sem_v.inc(i)


def alloc_l0ffn(b: Builder):
    st = {}
    st["w"] = b.sb("f0_w", [128, 3, NCH, 1024], BF16)
    st["mu"] = b.sb("f0_mu", [128, 4, NCH], F32)
    st["xf"] = b.sb("f0_xf", [128, 2, NCH, TB], F32)
    st["xln"] = b.sb("f0_xln", [128, NCH, T + 1], BF16)
    st["xr"] = b.sb("f0_xr", [128, NCH, TB], BF16)
    st["xk"] = b.sb("f0_xk", [128, NCH, TB], BF16)
    st["rg"] = b.sb("f0_rg", [128, NCH, TB], BF16)
    st["ka"] = b.sb("f0_ka", [128, NCH, TB], BF16)
    st["tmp"] = b.sb("f0_tmp", [128, 2, TB], F32)
    st["tmpb"] = b.sb("f0_tmpb", [128, TB], BF16)
    st["ln"] = LNState(b, "f0")
    st["mm_ps"] = b.ps("f0_mm", [128, 2, TB], F32)
    return st


def emit_l0ffn_weights(b: Builder, st):
    nc = b.nc
    s = b.sem("f0_wload")
    s.inc(nc.gpsimd.dma_start(st["w"][:, 0].rearrange("p c n -> p (c n)"), b.dram["f0_wr"]), 16)
    s.inc(nc.gpsimd.dma_start(st["w"][:, 1].rearrange("p c n -> p (c n)"), b.dram["f0_wk"]), 16)
    s.inc(nc.gpsimd.dma_start(st["w"][:, 2].rearrange("p c n -> p (c n)"), b.dram["f0_wv"]), 16)
    s.inc(nc.gpsimd.dma_start(st["mu"][:].rearrange("p a c -> p (a c)"), b.dram["f0_mu4"]), 16)
    return s


def emit_l0ffn(b: Builder, C, st, src, dst):
    nc = b.nc
    sem_t, sem_v, sem_s = b.sem("m_t"), b.sem("m_v"), b.sem("m_s")
    wload = b.sem("f0_wload")
    dx = [b.sem("f0_dx0"), b.sem("f0_dx1")]
    sx = [b.sem("f0_sx0"), b.sem("f0_sx1")]
    ln = st["ln"]
    bank = [Gate(), Gate()]
    tgate = [Gate(), Gate()]
    xf_gate = [Gate(), Gate()]

    nc.scalar.wait_ge(wload.h, wload.n)
    nc.scalar.activation(st["mu"][:, 1], st["mu"][:, 0], AF.Copy, scale=-1.0, bias=1.0)
    sem_s.inc(nc.scalar.activation(st["mu"][:, 3], st["mu"][:, 2], AF.Copy,
                                   scale=-1.0, bias=1.0))

    for s in range(BL):
        sem_v.inc(nc.vector.memset(st["xln"][:, :, 0:1], 0.0))
        for tb in range(NTB):
            par = tb % 2
            t0 = tb * TB
            xf = st["xf"][:, par]
            xf_gate[par].wait(nc.sync)
            src_blk = src[s, :, t0:t0 + TB].rearrange("(c p) t -> p c t", p=128)
            dx[par].inc(nc.sync.dma_start(xf[:, :, :], src_blk), 16)
            for eng in (nc.vector, nc.scalar, nc.tensor):
                eng.wait_ge(dx[par].h, dx[par].n)
            xch = [xf[:, c] for c in range(NCH)]

            emit_ln_stats(b, C, ln, xch, sem_t, sem_v, sem_s)
            xln_dsts = [st["xln"][:, c, 1 + t0:1 + t0 + TB] for c in range(NCH)]
            emit_ln_normalize(b, ln, xch, xln_dsts, st["tmp"][:, 0], sem_t, sem_v)

            for c in range(NCH):
                cur = st["xln"][:, c, 1 + t0:1 + t0 + TB]
                prv = st["xln"][:, c, t0:t0 + TB]
                nc.vector.tensor_single_scalar(st["tmpb"][:], prv, st["mu"][:, 1, c:c + 1], ALU.mult)
                nc.vector.scalar_tensor_tensor(st["xr"][:, c], cur, st["mu"][:, 0, c:c + 1],
                                               st["tmpb"][:], ALU.mult, ALU.add)
                nc.vector.tensor_single_scalar(st["tmpb"][:], prv, st["mu"][:, 3, c:c + 1], ALU.mult)
                i = nc.vector.scalar_tensor_tensor(st["xk"][:, c], cur, st["mu"][:, 2, c:c + 1],
                                                   st["tmpb"][:], ALU.mult, ALU.add)
            sem_v.inc(i)

            nc.tensor.wait_ge(sem_v.h, sem_v.n)
            nc.tensor.wait_ge(sem_s.h, sem_s.n)
            for oc in range(NCH):
                pp = oc % 2
                bank[pp].wait(nc.tensor)
                for kc in range(NCH):
                    mm = nc.tensor.matmul(st["mm_ps"][:, pp],
                                          st["w"][:, 0, kc, oc * 128:(oc + 1) * 128],
                                          st["xr"][:, kc], start=(kc == 0), stop=(kc == NCH - 1))
                sem_t.inc(mm)
                nc.scalar.wait_ge(sem_t.h, sem_t.n)
                sem_s.inc(nc.scalar.activation(st["rg"][:, oc], st["mm_ps"][:, pp], AF.Sigmoid))
                bank[pp].set(sem_s)

            for oc in range(NCH):
                pp = oc % 2
                bank[pp].wait(nc.tensor)
                for kc in range(NCH):
                    mm = nc.tensor.matmul(st["mm_ps"][:, pp],
                                          st["w"][:, 1, kc, oc * 128:(oc + 1) * 128],
                                          st["xk"][:, kc], start=(kc == 0), stop=(kc == NCH - 1))
                sem_t.inc(mm)
                nc.vector.wait_ge(sem_t.h, sem_t.n)
                tgate[pp].wait(nc.vector)
                sem_v.inc(nc.vector.tensor_single_scalar(st["tmp"][:, pp], st["mm_ps"][:, pp], 0.0, ALU.max))
                bank[pp].set(sem_v)
                nc.scalar.wait_ge(sem_v.h, sem_v.n)
                sem_s.inc(nc.scalar.activation(st["ka"][:, oc], st["tmp"][:, pp], AF.Square))
                tgate[pp].set(sem_s)

            nc.tensor.wait_ge(sem_s.h, sem_s.n)
            for oc in range(NCH):
                pp = oc % 2
                bank[pp].wait(nc.tensor)
                for kc in range(NCH):
                    mm = nc.tensor.matmul(st["mm_ps"][:, pp],
                                          st["w"][:, 2, kc, oc * 128:(oc + 1) * 128],
                                          st["ka"][:, kc], start=(kc == 0), stop=(kc == NCH - 1))
                sem_t.inc(mm)
                nc.vector.wait_ge(sem_t.h, sem_t.n)
                tgate[pp].wait(nc.vector)
                sem_v.inc(nc.vector.tensor_mul(st["tmp"][:, pp], st["rg"][:, oc], st["mm_ps"][:, pp]))
                bank[pp].set(sem_v)
                i = nc.vector.scalar_tensor_tensor(xf[:, oc], st["tmp"][:, pp], 0.5,
                                                   xf[:, oc], ALU.mult, ALU.add)
                sem_v.inc(i)
                tgate[pp].set(sem_v)

            nc.gpsimd.wait_ge(sem_v.h, sem_v.n)
            dst_blk = dst[s, :, t0:t0 + TB].rearrange("(c p) t -> p c t", p=128)
            sx[par].inc(nc.gpsimd.dma_start(dst_blk, xf[:, :, :]), 16)
            xf_gate[par].set(sx[par])
    return [sx[0], sx[1]]


def host_l0ffn_weights(inp):
    def tolayout(w):
        return np.ascontiguousarray(w.reshape(NCH, 128, D).transpose(1, 0, 2)).reshape(128, NCH * D)
    wr = tolayout(np.ascontiguousarray(inp["L0_ffn_W_r"].T))
    wk = tolayout(np.ascontiguousarray(inp["L0_ffn_W_k"].T))
    wv = tolayout(np.ascontiguousarray(inp["L0_ffn_W_v"].T))
    mur = inp["L0_ffn_mu_r"].reshape(NCH, 128).T
    muk = inp["L0_ffn_mu_k"].reshape(NCH, 128).T
    zeros = np.zeros_like(mur)
    mu4 = np.stack([mur, zeros, muk, zeros], axis=1).reshape(128, 4 * NCH)
    return {"f0_wr": wr, "f0_wk": wk, "f0_wv": wv, "f0_mu4": np.ascontiguousarray(mu4)}


NTT = T // 128


DEBUG_A0 = False


def emit_l0attn(b: Builder, C, src, dst):
    nc = b.nc
    sem_t, sem_v, sem_s = b.sem("m_t"), b.sem("m_v"), b.sem("m_s")
    dxa = b.sem("a0_dx")
    wla = b.sem("a0_wl")
    wlf = b.sem("a0_wlf")
    sxa = b.sem("a0_sx")
    H = 16

    with b.scope():
        xln = b.sb("a0_xln", [128, NCH, T], BF16)
        Q = b.sb("a0_Q", [128, NCH, T], BF16)
        Kp = b.sb("a0_K", [128, NCH, T], BF16)
        Vtm = b.sb("a0_V", [128, NTT, 1024], BF16)
        afm = b.sb("a0_afm", [128, NCH, T], BF16)

        for s in range(BL):
            with b.scope():
                ln = LNState(b, "a0ln")
                xdb = b.sb("a0_xdb", [128, NCH, TB], F32)
                tmp = b.sb("a0_tmp", [128, TB], F32)
                phase_gate(b, (nc.sync,))
                v_done = s_done = None
                for tb in range(NTB):
                    if v_done is not None:
                        nc.sync.wait_ge(sem_v.h, v_done)
                        nc.sync.wait_ge(sem_s.h, s_done)
                    blk = src[s, :, tb * TB:(tb + 1) * TB].rearrange("(c p) t -> p c t", p=128)
                    dxa.inc(nc.sync.dma_start(xdb[:], blk), 16)
                    for eng in (nc.vector, nc.scalar, nc.tensor):
                        eng.wait_ge(dxa.h, dxa.n)
                    xch = [xdb[:, c] for c in range(NCH)]
                    emit_ln_stats(b, C, ln, xch, sem_t, sem_v, sem_s)
                    dsts = [xln[:, c, tb * TB:(tb + 1) * TB] for c in range(NCH)]
                    emit_ln_normalize(b, ln, xch, dsts, tmp, sem_t, sem_v)
                    v_done, s_done = sem_v.n, sem_s.n

            with b.scope():
                wq = b.sb("a0_wq", [128, NCH, 1024], BF16)
                wk = b.sb("a0_wk", [128, NCH, 1024], BF16)
                bqk = b.sb("a0_bqk", [128, 2, NCH], F32)
                E2 = b.sb("a0_E2", [128, T], BF16)
                ps = b.ps("a0_pp", [128, 2, TB], F32)
                phase_gate(b, (nc.gpsimd, nc.sync))
                wla.inc(nc.gpsimd.dma_start(wq[:].rearrange("p c n -> p (c n)"), b.dram["a0_wq"]), 16)
                wla.inc(nc.gpsimd.dma_start(wk[:].rearrange("p c n -> p (c n)"), b.dram["a0_wk"]), 16)
                wla.inc(nc.gpsimd.dma_start(E2[:], b.dram["a0_E2"]), 16)
                wlf.inc(nc.sync.dma_start(bqk[:].rearrange("p a c -> p (a c)"), b.dram["a0_bqk"]), 16)
                nc.tensor.wait_ge(wla.h, wla.n)
                nc.tensor.wait_ge(sem_v.h, sem_v.n)
                nc.vector.wait_ge(wlf.h, wlf.n)
                nc.vector.wait_ge(wla.h, wla.n)
                nc.scalar.wait_ge(wlf.h, wlf.n)
                bank = [Gate(), Gate()]
                kgrp = 0
                for wsel, wbuf in ((0, wq), (1, wk)):
                    for oc in range(NCH):
                        for tb in range(NTB):
                            pp = kgrp % 2
                            kgrp += 1
                            bank[pp].wait(nc.tensor)
                            for kc in range(NCH):
                                mm = nc.tensor.matmul(ps[:, pp],
                                                      wbuf[:, kc, oc * 128:(oc + 1) * 128],
                                                      xln[:, kc, tb * TB:(tb + 1) * TB],
                                                      start=(kc == 0), stop=(kc == NCH - 1))
                            sem_t.inc(mm)
                            if wsel == 0:
                                nc.scalar.wait_ge(sem_t.h, sem_t.n)
                                sem_s.inc(nc.scalar.activation(
                                    Q[:, oc, tb * TB:(tb + 1) * TB], ps[:, pp], AF.Identity,
                                    bias=bqk[:, 0, oc:oc + 1]))
                                bank[pp].set(sem_s)
                            else:
                                nc.vector.wait_ge(sem_t.h, sem_t.n)
                                sem_v.inc(nc.vector.scalar_tensor_tensor(
                                    Kp[:, oc, tb * TB:(tb + 1) * TB], ps[:, pp],
                                    bqk[:, 1, oc:oc + 1], E2[:, tb * TB:(tb + 1) * TB],
                                    ALU.add, ALU.add))
                                bank[pp].set(sem_v)
                nc.tensor.wait_ge(sem_v.h, sem_v.n)
                nc.tensor.wait_ge(sem_s.h, sem_s.n)

            with b.scope():
                wv = b.sb("a0_wv", [128, NCH, 1024], BF16)
                ps = b.ps("a0_pv", [128, 2, TB], F32)
                phase_gate(b, (nc.gpsimd,))
                wla.inc(nc.gpsimd.dma_start(wv[:].rearrange("p c n -> p (c n)"), b.dram["a0_wv"]), 16)
                nc.tensor.wait_ge(wla.h, wla.n)
                nc.tensor.wait_ge(sem_v.h, sem_v.n)
                nc.tensor.wait_ge(sem_s.h, sem_s.n)
                bank = [Gate(), Gate()]
                k = 0
                for tt in range(NTT):
                    for dvb in range(2):
                        pp = k % 2
                        k += 1
                        bank[pp].wait(nc.tensor)
                        for kc in range(NCH):
                            mm = nc.tensor.matmul(ps[:, pp],
                                                  xln[:, kc, tt * 128:(tt + 1) * 128],
                                                  wv[:, kc, dvb * 512:(dvb + 1) * 512],
                                                  start=(kc == 0), stop=(kc == NCH - 1))
                        sem_t.inc(mm)
                        nc.scalar.wait_ge(sem_t.h, sem_t.n)
                        sem_s.inc(nc.scalar.copy(Vtm[:, tt, dvb * 512:(dvb + 1) * 512], ps[:, pp]))
                        bank[pp].set(sem_s)

            with b.scope():
                S_ps = b.ps("a0_S", [128, NTB, TB], F32)
                PT_ps = b.ps("a0_PT", [128, 2, 1024], BF16)
                av_ps = b.ps("a0_av", [128, 128], F32)
                P_sb = b.sb("a0_P", [128, T], BF16)
                PT_sb = b.sb("a0_PTs", [128, T], BF16)
                negm = b.sb("a0_negm", [128, 2], F32)
                denom = b.sb("a0_den", [128, 1], F32)
                recip = b.sb("a0_rec", [128, 1], F32)
                for h in range(H):
                    qrow = (h % 2) * 64
                    qch = h // 2
                    for qt in range(NTT):
                        nc.tensor.wait_ge(sem_s.h, sem_s.n)
                        nc.tensor.wait_ge(sem_v.h, sem_v.n)
                        for kb in range(NTB):
                            mm = nc.tensor.matmul(
                                S_ps[:, kb],
                                Q[qrow:qrow + 64, qch, qt * 128:(qt + 1) * 128],
                                Kp[qrow:qrow + 64, qch, kb * TB:(kb + 1) * TB],
                                start=True, stop=True)
                        sem_t.inc(mm)
                        nc.vector.wait_ge(sem_t.h, sem_t.n)
                        nc.vector.wait_ge(sem_s.h, sem_s.n)
                        vred(b, lambda: nc.vector.reduce_max(negm[:, 0:1], S_ps[:], axis=AX.XY, negate=True))
                        sem_v.inc(nc.vector.tensor_single_scalar(negm[:, 1:2], negm[:, 0:1], 0.125, ALU.mult))
                        nc.scalar.wait_ge(sem_v.h, sem_v.n)
                        nc.scalar.wait_ge(sem_t.h, sem_t.n)
                        sem_s.inc(nc.scalar.activation(
                            P_sb[:], S_ps[:].rearrange("p a b -> p (a b)"), AF.Exp,
                            scale=0.125, bias=negm[:, 1:2], accum_out=denom[:, 0:1]))
                        nc.vector.wait_ge(sem_s.h, sem_s.n)
                        vred(b, lambda: nc.vector.reciprocal(recip[:], denom[:]))
                        sem_v.inc(nc.vector.tensor_single_scalar(P_sb[:], P_sb[:],
                                                                 recip[:, 0:1], ALU.mult))
                        nc.tensor.wait_ge(sem_v.h, sem_v.n)
                        for j in range(NTT):
                            mm = nc.tensor.transpose(
                                PT_ps[:, j // 8, (j % 8) * 128:(j % 8 + 1) * 128],
                                P_sb[:, j * 128:(j + 1) * 128], C.ident_bf[:])
                        sem_t.inc(mm)
                        nc.vector.wait_ge(sem_t.h, sem_t.n)
                        nc.vector.tensor_copy(PT_sb[:, 0:1024], PT_ps[:, 0])
                        sem_v.inc(nc.vector.tensor_copy(PT_sb[:, 1024:2048], PT_ps[:, 1]))
                        nc.tensor.wait_ge(sem_v.h, sem_v.n)
                        for j in range(NTT):
                            mm = nc.tensor.matmul(
                                av_ps[0:64, :], Vtm[:, j, h * 64:(h + 1) * 64],
                                PT_sb[:, j * 128:(j + 1) * 128],
                                start=(j == 0), stop=(j == NTT - 1))
                        sem_t.inc(mm)
                        nc.vector.wait_ge(sem_t.h, sem_t.n)
                        sem_v.inc(nc.vector.tensor_copy(
                            afm[qrow:qrow + 64, qch, qt * 128:(qt + 1) * 128],
                            av_ps[0:64, :]))

            if DEBUG_A0 and s == 0:
                dbg = b.sem("a0_dbg")
                nc.gpsimd.wait_ge(b.sem("m_v").h, b.sem("m_v").n)
                nc.gpsimd.wait_ge(b.sem("m_s").h, b.sem("m_s").n)
                dbg.inc(nc.gpsimd.dma_start(b.dram["dbg_q"], Q[:].rearrange("p c t -> p (c t)")), 16)
                dbg.inc(nc.gpsimd.dma_start(b.dram["dbg_k"], Kp[:].rearrange("p c t -> p (c t)")), 16)
                dbg.inc(nc.gpsimd.dma_start(b.dram["dbg_v"], Vtm[:].rearrange("p c t -> p (c t)")), 16)
                dbg.inc(nc.gpsimd.dma_start(b.dram["dbg_afm"], afm[:].rearrange("p c t -> p (c t)")), 16)
                nc.gpsimd.wait_ge(dbg.h, dbg.n)

            with b.scope():
                wo = b.sb("a0_wo", [128, NCH, 1024], BF16)
                bvb = b.sb("a0_bv", [128, NCH], BF16)
                bo = b.sb("a0_bo", [128, NCH], F32)
                beff = b.sb("a0_beff", [128, NCH], F32)
                xdb = b.sb("a0_xdb2", [128, NCH, TB], F32)
                ps = b.ps("a0_po", [128, 2, TB], F32)
                phase_gate(b, (nc.gpsimd, nc.sync))
                wla.inc(nc.gpsimd.dma_start(wo[:].rearrange("p c n -> p (c n)"), b.dram["a0_wo"]), 16)
                wla.inc(nc.gpsimd.dma_start(bvb[:], b.dram["a0_bv"]), 16)
                wlf.inc(nc.sync.dma_start(bo[:], b.dram["a0_bo"]), 16)
                nc.tensor.wait_ge(wla.h, wla.n)
                nc.tensor.wait_ge(sem_v.h, sem_v.n)
                nc.tensor.wait_ge(sem_s.h, sem_s.n)
                for oc in range(NCH):
                    nc.tensor.wait_ge(sem_v.h, sem_v.n)
                    for kc in range(NCH):
                        mm = nc.tensor.matmul(ps[:, 0, 0:1], wo[:, kc, oc * 128:(oc + 1) * 128],
                                              bvb[:, kc:kc + 1], start=(kc == 0), stop=(kc == NCH - 1))
                    sem_t.inc(mm)
                    nc.vector.wait_ge(sem_t.h, sem_t.n)
                    nc.vector.wait_ge(wlf.h, wlf.n)
                    sem_v.inc(nc.vector.tensor_add(beff[:, oc:oc + 1], ps[:, 0, 0:1],
                                                   bo[:, oc:oc + 1]))
                nc.tensor.wait_ge(sem_v.h, sem_v.n)
                store_chk = None
                bank = [Gate(), Gate()]
                for tb in range(NTB):
                    if store_chk is not None:
                        nc.sync.wait_ge(sxa.h, store_chk)
                    blk = src[s, :, tb * TB:(tb + 1) * TB].rearrange("(c p) t -> p c t", p=128)
                    dxa.inc(nc.sync.dma_start(xdb[:], blk), 16)
                    for eng in (nc.vector, nc.tensor):
                        eng.wait_ge(dxa.h, dxa.n)
                    for oc in range(NCH):
                        pp = oc % 2
                        bank[pp].wait(nc.tensor)
                        for kc in range(NCH):
                            mm = nc.tensor.matmul(ps[:, pp],
                                                  wo[:, kc, oc * 128:(oc + 1) * 128],
                                                  afm[:, kc, tb * TB:(tb + 1) * TB],
                                                  start=(kc == 0), stop=(kc == NCH - 1))
                        sem_t.inc(mm)
                        nc.vector.wait_ge(sem_t.h, sem_t.n)
                        sem_v.inc(nc.vector.scalar_tensor_tensor(
                            xdb[:, oc], ps[:, pp], beff[:, oc:oc + 1], xdb[:, oc],
                            ALU.add, ALU.add))
                        bank[pp].set(sem_v)
                    nc.gpsimd.wait_ge(sem_v.h, sem_v.n)
                    dblk = dst[s, :, tb * TB:(tb + 1) * TB].rearrange("(c p) t -> p c t", p=128)
                    sxa.inc(nc.gpsimd.dma_start(dblk, xdb[:]), 16)
                    store_chk = sxa.n
    return [sxa]


def host_l0attn_weights(inp):
    def tolayout(w):
        return np.ascontiguousarray(w.reshape(NCH, 128, D).transpose(1, 0, 2)).reshape(128, NCH * D)
    Er = inp["L0_attn_Er"]
    E = Er[2952:5000][::-1].T
    E2 = np.ascontiguousarray(np.concatenate([E, E], axis=0))
    bq = inp["L0_attn_b_q"].reshape(NCH, 128).T
    bk = inp["L0_attn_b_k"].reshape(NCH, 128).T
    bqk = np.ascontiguousarray(np.stack([bq, bk], axis=1).reshape(128, 2 * NCH))
    return {
        "a0_wq": tolayout(np.ascontiguousarray(inp["L0_attn_W_q"].T)),
        "a0_wk": tolayout(np.ascontiguousarray(inp["L0_attn_W_k"].T)),
        "a0_wv": tolayout(np.ascontiguousarray(inp["L0_attn_W_v"].T)),
        "a0_wo": tolayout(np.ascontiguousarray(inp["L0_attn_W_o"].T)),
        "a0_E2": E2,
        "a0_bqk": bqk,
        "a0_bv": np.ascontiguousarray(inp["L0_attn_b_v"].reshape(NCH, 128).T),
        "a0_bo": np.ascontiguousarray(inp["L0_attn_b_o"].reshape(NCH, 128).T),
    }


NE = 32
DFF = 1024
DEBUG_F1 = False
DBG_QT, DBG_H = 5, 9


def emit_rms_rows(b, C, ln, xch, wcol, dsts, sem_t, sem_v, sem_s):
    nc = b.nc
    for c in range(NCH):
        isq = nc.scalar.square(ln.sq[:, c], xch[c])
    sem_s.inc(isq)
    nc.tensor.wait_ge(sem_s.h, sem_s.n)
    for c in range(NCH):
        mm = nc.tensor.matmul(ln.stat_ps[0:1, 0], C.ones_bf[:], ln.sq[:, c],
                              start=(c == 0), stop=(c == NCH - 1))
    sem_t.inc(mm)
    nc.vector.wait_ge(sem_t.h, sem_t.n)
    sem_v.inc(nc.vector.tensor_single_scalar(ln.rrow[:], ln.stat_ps[0:1, 0], 1.0 / D, ALU.mult))
    nc.scalar.wait_ge(sem_v.h, sem_v.n)
    nc.scalar.activation(ln.rrow[:], ln.rrow[:], AF.Ln, bias=EPS_LN)
    sem_s.inc(nc.scalar.activation(ln.rrow[:], ln.rrow[:], AF.Exp, scale=-0.5))
    nc.tensor.wait_ge(sem_s.h, sem_s.n)
    nc.tensor.wait_ge(sem_v.h, sem_v.n)
    sem_t.inc(nc.tensor.matmul(ln.bc_ps[:, 0], C.ones1f[:], ln.rrow[:], start=True, stop=True))
    nc.vector.wait_ge(sem_t.h, sem_t.n)
    for c in range(NCH):
        iv = nc.vector.scalar_tensor_tensor(dsts[c], xch[c], wcol[:, c:c + 1],
                                            ln.bc_ps[:, 0], ALU.mult, ALU.mult)
    sem_v.inc(iv)


def emit_l1ffn(b: Builder, C, src, dst):
    nc = b.nc
    sem_t, sem_v, sem_s = b.sem("m_t"), b.sem("m_v"), b.sem("m_s")
    dxf = b.sem("f1_dx")
    wlf = b.sem("f1_wl")
    wls = b.sem("f1_wls")
    sxf = b.sem("f1_sx")
    H, KV, DH = 16, 4, 64
    xmid = b.dram["f1_mid"]

    with b.scope():
        h2 = b.sb("f1_h2", [128, NCH, T], BF16)
        ce = b.sb("f1_ce", [128, NTT, NE], F32)
        yfm = b.sb("f1_yfm", [128, NCH, T], BF16)
        if DEBUG_F1:
            dbg_msk = b.sb("f1_dmsk", [128, 256], F32)
            dbg_P = b.sb("f1_dP", [128, 256], BF16)
            dbg_cols = b.sb("f1_dcols", [128, 8], F32)

        for s in range(BL):
            with b.scope():
                qfm = b.sb("f1_qfm", [128, NCH, T], BF16)
                kfm = b.sb("f1_kfm", [128, KV, T], BF16)
                vtm = b.sb("f1_vtm", [128, NTT, KV * DH], BF16)

                h_full = b.sb("f1_hf", [128, NCH, T], BF16)

                with b.scope():
                    ln = LNState(b, "f1ln")
                    xdb = b.sb("f1_xdb", [128, NCH, TB], F32)
                    anw = b.sb("f1_anw", [128, NCH], F32)
                    ltmp = b.sb("f1_ltmp", [128, TB], F32)
                    phase_gate(b, (nc.gpsimd, nc.sync))
                    wls.inc(nc.sync.dma_start(anw[:], b.dram["f1_anw"]), 16)
                    spill_chk = None
                    for tb in range(NTB):
                        t0 = tb * TB
                        if spill_chk is not None:
                            nc.sync.wait_ge(sxf.h, spill_chk)
                        blk = src[s, :, t0:t0 + TB].rearrange("(c p) t -> p c t", p=128)
                        dxf.inc(nc.sync.dma_start(xdb[:], blk), 16)
                        for eng in (nc.vector, nc.scalar, nc.tensor):
                            eng.wait_ge(dxf.h, dxf.n)
                        xch = [xdb[:, c] for c in range(NCH)]
                        emit_ln_stats(b, C, ln, xch, sem_t, sem_v, sem_s)
                        emit_ln_normalize(b, ln, xch, xch, ltmp, sem_t, sem_v)
                        if tb == 0:
                            nc.vector.wait_ge(wls.h, wls.n)
                        nc.scalar.wait_ge(sem_v.h, sem_v.n)
                        emit_rms_rows(b, C, ln, xch, anw,
                                      [h_full[:, c, t0:t0 + TB] for c in range(NCH)],
                                      sem_t, sem_v, sem_s)
                        nc.gpsimd.wait_ge(sem_v.h, sem_v.n)
                        nc.gpsimd.wait_ge(sem_s.h, sem_s.n)
                        mblk = xmid[s, :, t0:t0 + TB].rearrange("(c p) t -> p c t", p=128)
                        sxf.inc(nc.gpsimd.dma_start(mblk, xdb[:]), 16)
                        spill_chk = sxf.n

                with b.scope():
                    qkvw = b.sb("f1_qkvw", [128, NCH, 1536], BF16)
                    qkvb = b.sb("f1_qkvb", [128, 1536], BF16)
                    qkvt = b.sb("f1_qkvt", [128, 1536], F32)
                    qktm = b.sb("f1_qktm", [128, 1280], BF16)
                    rtmp = b.sb("f1_rt", [128, 2, TB], F32)
                    sinc = b.sb("f1_sin", [128, 2, NTT, 32], F32)
                    qkv_ps = b.ps("f1_qkv", [128, 3, TB], F32)
                    tr_ps = b.ps("f1_tr", [128, 2, 1024], BF16)
                    phase_gate(b, (nc.gpsimd, nc.sync))
                    wlf.inc(nc.gpsimd.dma_start(qkvw[:].rearrange("p c n -> p (c n)"),
                                                b.dram["f1_qkvw"]), 16)
                    wlf.inc(nc.gpsimd.dma_start(qkvb[:], b.dram["f1_qkvb"]), 16)
                    wls.inc(nc.sync.dma_start(sinc[:].rearrange("p a c n -> p (a c n)"),
                                              b.dram["f1_sincos"]), 16)
                    nc.tensor.wait_ge(wlf.h, wlf.n)
                    nc.tensor.wait_ge(sem_v.h, sem_v.n)
                    nc.vector.wait_ge(wlf.h, wlf.n)
                    nc.vector.wait_ge(wls.h, wls.n)
                    for tt in range(NTT):
                        if tt:
                            nc.tensor.wait_ge(sem_v.h, v_evac)
                        for p3 in range(3):
                            for kc in range(NCH):
                                mm = nc.tensor.matmul(
                                    qkv_ps[:, p3], h_full[:, kc, tt * 128:(tt + 1) * 128],
                                    qkvw[:, kc, p3 * TB:(p3 + 1) * TB],
                                    start=(kc == 0), stop=(kc == NCH - 1))
                        sem_t.inc(mm)
                        nc.vector.wait_ge(sem_t.h, sem_t.n)
                        for p3 in range(3):
                            iv = nc.vector.tensor_add(
                                qkvt[:, p3 * TB:(p3 + 1) * TB], qkv_ps[:, p3],
                                qkvb[:, p3 * TB:(p3 + 1) * TB])
                        v_evac = sem_v.n + 1
                        sem_v.inc(iv)
                        sem_v.inc(nc.vector.tensor_copy(vtm[:, tt, :], qkvt[:, 1280:1536]))
                        sin_q = bass.AP(sinc, (0 * NTT + tt) * 32, [[2 * NTT * 32, 128], [0, 16], [1, 32]])
                        cos_q = bass.AP(sinc, (1 * NTT + tt) * 32, [[2 * NTT * 32, 128], [0, 16], [1, 32]])
                        sin_k = bass.AP(sinc, (0 * NTT + tt) * 32, [[2 * NTT * 32, 128], [0, 4], [1, 32]])
                        cos_k = bass.AP(sinc, (1 * NTT + tt) * 32, [[2 * NTT * 32, 128], [0, 4], [1, 32]])
                        for (o0, o1, w_, sn, cs) in ((0, 1024, 512, sin_q, cos_q),
                                                     (1024, 1280, 128, sin_k, cos_k)):
                            ev = qkvt[:, o0:o1:2]
                            od = qkvt[:, o0 + 1:o1:2]
                            nc.vector.tensor_mul(rtmp[:, 0, 0:w_], od, sn)
                            nc.vector.tensor_mul(rtmp[:, 1, 0:w_], ev, cs)
                            nc.vector.tensor_sub(qktm[:, o0:o1:2], rtmp[:, 1, 0:w_], rtmp[:, 0, 0:w_])
                            nc.vector.tensor_mul(rtmp[:, 0, 0:w_], ev, sn)
                            nc.vector.tensor_mul(rtmp[:, 1, 0:w_], od, cs)
                            iv = nc.vector.tensor_add(qktm[:, o0 + 1:o1:2], rtmp[:, 1, 0:w_], rtmp[:, 0, 0:w_])
                        sem_v.inc(iv)
                        nc.tensor.wait_ge(sem_v.h, sem_v.n)
                        if tt:
                            nc.tensor.wait_ge(sem_v.h, tr_evac)
                        for c in range(NCH):
                            nc.tensor.transpose(tr_ps[:, 0, c * 128:(c + 1) * 128],
                                                qktm[:, c * 128:(c + 1) * 128], C.ident_bf[:])
                        for g in range(KV):
                            kc_ap = qktm[:, 1024 + g * 64:1024 + (g + 1) * 64]
                            nc.tensor.transpose(tr_ps[0:64, 1, g * 128:g * 128 + 128], kc_ap,
                                                C.ident_bf[:])
                            mm = nc.tensor.transpose(tr_ps[64:128, 1, g * 128:g * 128 + 128],
                                                     kc_ap, C.ident_bf[:])
                        sem_t.inc(mm)
                        nc.vector.wait_ge(sem_t.h, sem_t.n)
                        nc.vector.tensor_copy(qfm[:, :, tt * 128:(tt + 1) * 128],
                                              tr_ps[:, 0].rearrange("p (c q) -> p c q", q=128))
                        tr_evac = sem_v.n + 1
                        sem_v.inc(nc.vector.tensor_copy(
                            kfm[:, :, tt * 128:(tt + 1) * 128],
                            tr_ps[:, 1, 0:512].rearrange("p (c q) -> p c q", q=128)))

                if DEBUG_F1 and s == 0:
                    dbg = b.sem("f1_dbg")
                    for eng in (nc.gpsimd,):
                        eng.wait_ge(sem_v.h, sem_v.n)
                        eng.wait_ge(sem_s.h, sem_s.n)
                    dbg.inc(nc.gpsimd.dma_start(b.dram["dbgf_h"], h_full[:].rearrange("p c t -> p (c t)")), 16)
                    dbg.inc(nc.gpsimd.dma_start(b.dram["dbgf_q"], qfm[:].rearrange("p c t -> p (c t)")), 16)
                    dbg.inc(nc.gpsimd.dma_start(b.dram["dbgf_k"], kfm[:].rearrange("p c t -> p (c t)")), 16)
                    dbg.inc(nc.gpsimd.dma_start(b.dram["dbgf_v"], vtm[:].rearrange("p c t -> p (c t)")), 16)
                    nc.gpsimd.wait_ge(dbg.h, dbg.n)

                with b.scope():
                    maskb = b.sb("f1_mask", [128, 256], BF16)
                    mask0 = b.sb("f1_mask0", [128, 128], BF16)
                    sinkt = b.sb("f1_sink", [128, H], F32)
                    sink8 = b.sb("f1_sink8", [128, H], F32)
                    msk = b.sb("f1_msk", [128, 272], F32)
                    Pq = b.sb("f1_P", [128, 272], BF16)
                    PTq = b.sb("f1_PT", [128, 256], BF16)
                    cols = b.sb("f1_cols", [128, 8], F32)
                    S_ps = b.ps("f1_S", [128, 256], F32)
                    PT_ps = b.ps("f1_PTp", [128, 256], BF16)
                    y_ps = b.ps("f1_y", [128, 128], F32)
                    phase_gate(b, (nc.gpsimd, nc.sync))
                    wlf.inc(nc.gpsimd.dma_start(maskb[:], b.dram["f1_mask"]), 16)
                    wlf.inc(nc.gpsimd.dma_start(mask0[:], b.dram["f1_mask0"]), 16)
                    wls.inc(nc.sync.dma_start(sinkt[:], b.dram["f1_sink"]), 16)
                    nc.vector.wait_ge(wlf.h, wlf.n)
                    nc.scalar.wait_ge(wls.h, wls.n)
                    sem_s.inc(nc.scalar.mul(sink8[:], sinkt[:], 8.0))
                    for qt in range(NTT):
                        nkb = 1 if qt == 0 else 2
                        kw = 128 * nkb
                        kt0 = 0 if qt == 0 else (qt - 1) * 128
                        for h in range(H):
                            g = h // 4
                            qrow, qch = (h % 2) * 64, h // 2
                            nc.tensor.wait_ge(sem_v.h, sem_v.n)
                            mm = nc.tensor.matmul(
                                S_ps[:, 0:kw],
                                qfm[qrow:qrow + 64, qch, qt * 128:(qt + 1) * 128],
                                kfm[qrow:qrow + 64, g, kt0:kt0 + kw],
                                start=True, stop=True)
                            sem_t.inc(mm)
                            nc.vector.wait_ge(sem_t.h, sem_t.n)
                            mk = mask0 if qt == 0 else maskb
                            nc.vector.tensor_add(msk[:, 0:kw], S_ps[:, 0:kw], mk[:, 0:kw])
                            vred(b, lambda: nc.vector.reduce_max(cols[:, 0:1], msk[:, 0:kw],
                                                          axis=AX.X, negate=True))
                            sem_v.inc(nc.vector.tensor_single_scalar(cols[:, 1:2], cols[:, 0:1],
                                                                     0.125, ALU.mult))
                            nc.scalar.copy(msk[:, kw:kw + 1], sink8[:, h:h + 1])
                            nc.scalar.wait_ge(sem_v.h, sem_v.n)
                            sem_s.inc(nc.scalar.activation(Pq[:, 0:kw + 1], msk[:, 0:kw + 1],
                                                           AF.Exp, scale=0.125, bias=cols[:, 1:2],
                                                           accum_out=cols[:, 2:3]))
                            nc.vector.wait_ge(sem_s.h, sem_s.n)
                            vred(b, lambda: nc.vector.reciprocal(cols[:, 5:6], cols[:, 2:3]))
                            nc.scalar.wait_ge(sem_v.h, sem_v.n)
                            sem_s.inc(nc.scalar.activation(Pq[:, 0:kw], Pq[:, 0:kw], AF.Identity,
                                                           scale=cols[:, 5:6]))
                            nc.tensor.wait_ge(sem_s.h, sem_s.n)
                            nc.tensor.wait_ge(sem_v.h, sem_v.n)
                            for kb in range(nkb):
                                mm = nc.tensor.transpose(PT_ps[:, kb * 128:(kb + 1) * 128],
                                                         Pq[:, kb * 128:(kb + 1) * 128],
                                                         C.ident_bf[:])
                            sem_t.inc(mm)
                            nc.vector.wait_ge(sem_t.h, sem_t.n)
                            sem_v.inc(nc.vector.tensor_copy(PTq[:, 0:kw], PT_ps[:, 0:kw]))
                            nc.tensor.wait_ge(sem_v.h, sem_v.n)
                            for kb in range(nkb):
                                ktile = (0 if qt == 0 else qt - 1) + kb
                                mm = nc.tensor.matmul(
                                    y_ps[0:64, :], vtm[:, ktile, g * 64:(g + 1) * 64],
                                    PTq[:, kb * 128:(kb + 1) * 128],
                                    start=(kb == 0), stop=(kb == nkb - 1))
                            sem_t.inc(mm)
                            nc.vector.wait_ge(sem_t.h, sem_t.n)
                            sem_v.inc(nc.vector.tensor_copy(
                                yfm[qrow:qrow + 64, qch, qt * 128:(qt + 1) * 128],
                                y_ps[0:64, :]))
                            if DEBUG_F1 and s == 0 and qt == DBG_QT and h == DBG_H:
                                nc.vector.tensor_copy(dbg_msk[:], msk[:, 0:256])
                                nc.vector.tensor_copy(dbg_P[:], Pq[:, 0:256])
                                sem_v.inc(nc.vector.tensor_copy(dbg_cols[:], cols[:]))

            if DEBUG_F1 and s == 0:
                dbg = b.sem("f1_dbg")
                nc.gpsimd.wait_ge(sem_v.h, sem_v.n)
                dbg.inc(nc.gpsimd.dma_start(b.dram["dbgf_y"], yfm[:].rearrange("p c t -> p (c t)")), 16)
                dbg.inc(nc.gpsimd.dma_start(b.dram["dbgf_msk"], dbg_msk[:]), 16)
                dbg.inc(nc.gpsimd.dma_start(b.dram["dbgf_P"], dbg_P[:]), 16)
                dbg.inc(nc.gpsimd.dma_start(b.dram["dbgf_cols"], dbg_cols[:]), 16)
                nc.gpsimd.wait_ge(dbg.h, dbg.n)

            with b.scope():
                ln = LNState(b, "f1ln2")
                projw = b.sb("f1_projw", [128, NCH, 1024], BF16)
                pbcol = b.sb("f1_pb", [128, NCH], F32)
                fnw = b.sb("f1_fnw", [128, NCH], F32)
                gatew = b.sb("f1_gatew", [128, NCH, NE], BF16)
                gatebt = b.sb("f1_gateb", [128, NE], F32)
                xsb = b.sb("f1_xsb", [128, NCH, TB], F32)
                g0 = b.sb("f1_g0", [128, NE], F32)
                g1 = b.sb("f1_g1", [128, NE], F32)
                ge = b.sb("f1_ge", [128, NE], F32)
                eb = b.sb("f1_eb", [128, NE], F32)
                cols = b.sb("f1_cols2", [128, 8], F32)
                ltmp = b.sb("f1_ltmp2", [128, TB], F32)
                pj_ps = b.ps("f1_pj", [128, 2, TB], F32)
                gt_ps = b.ps("f1_gt", [128, NE], F32)
                phase_gate(b, (nc.gpsimd, nc.sync))
                wlf.inc(nc.gpsimd.dma_start(projw[:].rearrange("p c n -> p (c n)"),
                                            b.dram["f1_projw"]), 16)
                wlf.inc(nc.gpsimd.dma_start(gatew[:].rearrange("p c n -> p (c n)"),
                                            b.dram["f1_gatew"]), 16)
                wls.inc(nc.sync.dma_start(pbcol[:], b.dram["f1_projb"]), 16)
                wls.inc(nc.sync.dma_start(fnw[:], b.dram["f1_fnw"]), 16)
                wls.inc(nc.sync.dma_start(gatebt[:], b.dram["f1_gateb"]), 16)
                spill_chk = None
                bank = [Gate(), Gate()]
                for tb in range(NTB):
                    t0 = tb * TB
                    if spill_chk is not None:
                        nc.sync.wait_ge(sxf.h, spill_chk)
                    mblk = xmid[s, :, t0:t0 + TB].rearrange("(c p) t -> p c t", p=128)
                    dxf.inc(nc.sync.dma_start(xsb[:], mblk), 16)
                    for eng in (nc.vector, nc.tensor):
                        eng.wait_ge(dxf.h, dxf.n)
                    nc.tensor.wait_ge(wlf.h, wlf.n)
                    nc.tensor.wait_ge(sem_v.h, sem_v.n)
                    nc.vector.wait_ge(wls.h, wls.n)
                    for oc in range(NCH):
                        pp = oc % 2
                        bank[pp].wait(nc.tensor)
                        for kc in range(NCH):
                            mm = nc.tensor.matmul(pj_ps[:, pp],
                                                  projw[:, kc, oc * 128:(oc + 1) * 128],
                                                  yfm[:, kc, t0:t0 + TB],
                                                  start=(kc == 0), stop=(kc == NCH - 1))
                        sem_t.inc(mm)
                        nc.vector.wait_ge(sem_t.h, sem_t.n)
                        sem_v.inc(nc.vector.scalar_tensor_tensor(
                            xsb[:, oc], pj_ps[:, pp], pbcol[:, oc:oc + 1],
                            xsb[:, oc], ALU.add, ALU.add))
                        bank[pp].set(sem_v)
                    nc.scalar.wait_ge(sem_v.h, sem_v.n)
                    emit_rms_rows(b, C, ln, [xsb[:, c] for c in range(NCH)], fnw,
                                  [h2[:, c, t0:t0 + TB] for c in range(NCH)],
                                  sem_t, sem_v, sem_s)
                    nc.tensor.wait_ge(sem_v.h, sem_v.n)
                    nc.tensor.wait_ge(sem_s.h, sem_s.n)
                    for i in range(4):
                        tt = tb * 4 + i
                        if tt > 0:
                            nc.tensor.wait_ge(sem_v.h, g0_read)
                        for kc in range(NCH):
                            mm = nc.tensor.matmul(gt_ps[:], h2[:, kc, tt * 128:(tt + 1) * 128],
                                                  gatew[:, kc, :],
                                                  start=(kc == 0), stop=(kc == NCH - 1))
                        sem_t.inc(mm)
                        nc.vector.wait_ge(sem_t.h, sem_t.n)
                        g0_read = sem_v.n + 1
                        vchain(b, lambda: nc.vector.tensor_add(g0[:], gt_ps[:], gatebt[:]))
                        vred(b, lambda: nc.vector.reduce_max(cols[:, 0:1], g0[:], axis=AX.X))
                        vchain(b, lambda: nc.vector.tensor_single_scalar(ge[:], g0[:], cols[:, 0:1], ALU.is_ge))
                        vchain(b, lambda: nc.vector.scalar_tensor_tensor(g1[:], ge[:], -1e30, g0[:], ALU.mult, ALU.add))
                        for it in range(2):
                            vred(b, lambda: nc.vector.reduce_max(cols[:, 1:2], g1[:], axis=AX.X))
                            vchain(b, lambda: nc.vector.tensor_single_scalar(ge[:], g1[:], cols[:, 1:2], ALU.is_ge))
                            vchain(b, lambda: nc.vector.scalar_tensor_tensor(g1[:], ge[:], -1e30, g1[:], ALU.mult, ALU.add))
                        vred(b, lambda: nc.vector.reduce_max(cols[:, 3:4], g1[:], axis=AX.X))
                        vchain(b, lambda: nc.vector.tensor_single_scalar(cols[:, 4:5], cols[:, 0:1],
                                                                         -1.0, ALU.mult))
                        nc.scalar.wait_ge(sem_v.h, sem_v.n)
                        sem_s.inc(nc.scalar.activation(eb[:], g0[:], AF.Exp, bias=cols[:, 4:5]))
                        vchain(b, lambda: nc.vector.tensor_single_scalar(ge[:], g0[:], cols[:, 3:4], ALU.is_ge))
                        nc.vector.wait_ge(sem_s.h, sem_s.n)
                        vchain(b, lambda: nc.vector.tensor_mul(eb[:], eb[:], ge[:]))
                        vred(b, lambda: nc.vector.reduce_sum(cols[:, 5:6], eb[:], axis=AX.X))
                        vred(b, lambda: nc.vector.reciprocal(cols[:, 6:7], cols[:, 5:6]))
                        sem_v.inc(nc.vector.tensor_single_scalar(ce[:, tt, :], eb[:],
                                                                 cols[:, 6:7], ALU.mult))
                    nc.gpsimd.wait_ge(sem_v.h, sem_v.n)
                    nc.gpsimd.wait_ge(sem_s.h, sem_s.n)
                    sxf.inc(nc.gpsimd.dma_start(mblk, xsb[:]), 16)
                    spill_chk = sxf.n

            if DEBUG_F1 and s == 0:
                dbg = b.sem("f1_dbg")
                nc.gpsimd.wait_ge(sem_v.h, sem_v.n)
                dbg.inc(nc.gpsimd.dma_start(b.dram["dbgf_h2"], h2[:].rearrange("p c t -> p (c t)")), 16)
                dbg.inc(nc.gpsimd.dma_start(b.dram["dbgf_ce"], ce[:].rearrange("p c t -> p (c t)")), 16)
                dbg.inc(nc.gpsimd.dma_start(b.dram["dbgf_g0"], g0[:]), 16)
                dbg.inc(nc.gpsimd.dma_start(b.dram["dbgf_g1"], g1[:]), 16)
                dbg.inc(nc.gpsimd.dma_start(b.dram["dbgf_eb"], eb[:]), 16)
                dbg.inc(nc.gpsimd.dma_start(b.dram["dbgf_ge"], ge[:]), 16)
                dbg.inc(nc.gpsimd.dma_start(b.dram["dbgf_gcols"], cols[:]), 16)
                nc.gpsimd.wait_ge(dbg.h, dbg.n)

            with b.scope():
                acc = b.sb("f1_acc", [128, NTT, 1024], F32)
                with b.scope():
                    wup = b.sb("f1_wup", [128, NCH, 2 * DFF], BF16)
                    wdn = b.sb("f1_wdn", [128, NCH, 1024], BF16)
                    act = b.sb("f1_act", [128, 2, NCH, TB], BF16)
                    stmp = b.sb("f1_stmp", [128, TB], F32)
                    up_ps = b.ps("f1_up", [128, 4, TB], F32)
                    dn_ps = b.ps("f1_dn", [128, 2, TB], F32)
                    upg = [Gate(), Gate(), Gate(), Gate()]
                    dng = [Gate(), Gate()]
                    actg = [Gate(), Gate()]
                    wg = Gate()
                    for e in range(NE):
                        wg.wait(nc.gpsimd)
                        phase_gate(b, (nc.gpsimd,)) if e == 0 else None
                        wlf.inc(nc.gpsimd.dma_start(wup[:].rearrange("p c n -> p (c n)"),
                                                    b.dram["f1_wup"][e]), 16)
                        wlf.inc(nc.gpsimd.dma_start(wdn[:].rearrange("p c n -> p (c n)"),
                                                    b.dram["f1_wdn"][e]), 16)
                        nc.tensor.wait_ge(wlf.h, wlf.n)
                        if e == 0:
                            nc.tensor.wait_ge(sem_v.h, sem_v.n)
                            nc.tensor.wait_ge(sem_s.h, sem_s.n)
                        for tb in range(NTB):
                            t0 = tb * TB
                            ap_ = tb % 2
                            actg[ap_].wait(nc.vector)
                            for j in range(NCH):
                                pu, pg_ = (j % 2) * 2, (j % 2) * 2 + 1
                                upg[pu].wait(nc.tensor)
                                for kc in range(NCH):
                                    mm = nc.tensor.matmul(up_ps[:, pu],
                                                          wup[:, kc, j * 128:(j + 1) * 128],
                                                          h2[:, kc, t0:t0 + TB],
                                                          start=(kc == 0), stop=(kc == NCH - 1))
                                sem_t.inc(mm)
                                upg[pg_].wait(nc.tensor)
                                for kc in range(NCH):
                                    mm = nc.tensor.matmul(up_ps[:, pg_],
                                                          wup[:, kc, DFF + j * 128:DFF + (j + 1) * 128],
                                                          h2[:, kc, t0:t0 + TB],
                                                          start=(kc == 0), stop=(kc == NCH - 1))
                                sem_t.inc(mm)
                                nc.scalar.wait_ge(sem_t.h, sem_t.n)
                                sem_s.inc(nc.scalar.activation(stmp[:], up_ps[:, pg_], AF.Silu))
                                upg[pg_].set(sem_s)
                                nc.vector.wait_ge(sem_s.h, sem_s.n)
                                iv = nc.vector.tensor_mul(act[:, ap_, j], stmp[:], up_ps[:, pu])
                                sem_v.inc(iv)
                                upg[pu].set(sem_v)
                            nc.tensor.wait_ge(sem_v.h, sem_v.n)
                            for i in range(4):
                                tt = tb * 4 + i
                                for dh in range(2):
                                    pd = dh
                                    dng[pd].wait(nc.tensor)
                                    for j in range(NCH):
                                        mm = nc.tensor.matmul(
                                            dn_ps[:, pd],
                                            act[:, ap_, j, i * 128:(i + 1) * 128],
                                            wdn[:, j, dh * TB:(dh + 1) * TB],
                                            start=(j == 0), stop=(j == NCH - 1))
                                    sem_t.inc(mm)
                                    nc.vector.wait_ge(sem_t.h, sem_t.n)
                                    if e == 0:
                                        iv = nc.vector.tensor_single_scalar(
                                            acc[:, tt, dh * TB:(dh + 1) * TB], dn_ps[:, pd],
                                            ce[:, tt, 0:1], ALU.mult)
                                    else:
                                        iv = nc.vector.scalar_tensor_tensor(
                                            acc[:, tt, dh * TB:(dh + 1) * TB], dn_ps[:, pd],
                                            ce[:, tt, e:e + 1],
                                            acc[:, tt, dh * TB:(dh + 1) * TB],
                                            ALU.mult, ALU.add)
                                    sem_v.inc(iv)
                                    dng[pd].set(sem_v)
                            actg[ap_].set(sem_t)
                        wg.set(sem_t)

                if DEBUG_F1 and s == 0:
                    dbg = b.sem("f1_dbg")
                    nc.gpsimd.wait_ge(sem_v.h, sem_v.n)
                    dbg.inc(nc.gpsimd.dma_start(b.dram["dbgf_acc"], acc[:].rearrange("p c t -> p (c t)")), 16)
                    nc.gpsimd.wait_ge(dbg.h, dbg.n)

                with b.scope():
                    accfm = b.sb("f1_accfm", [128, NCH, TB], F32)
                    xdb = b.sb("f1_xdb2", [128, NCH, TB], F32)
                    xmb = b.sb("f1_xmb", [128, NCH, TB], F32)
                    ftmp = b.sb("f1_ftmp", [128, TB], F32)
                    at_ps = b.ps("f1_at", [128, 2, TB], F32)
                    phase_gate(b, (nc.sync,))
                    store_chk = None
                    for tb in range(NTB):
                        t0 = tb * TB
                        if store_chk is not None:
                            nc.sync.wait_ge(sxf.h, store_chk)
                        blk = src[s, :, t0:t0 + TB].rearrange("(c p) t -> p c t", p=128)
                        mblk = xmid[s, :, t0:t0 + TB].rearrange("(c p) t -> p c t", p=128)
                        dxf.inc(nc.sync.dma_start(xdb[:], blk), 16)
                        dxf.inc(nc.sync.dma_start(xmb[:], mblk), 16)
                        nc.tensor.wait_ge(sem_v.h, sem_v.n)
                        for i in range(4):
                            tt = tb * 4 + i
                            if tb or i:
                                nc.tensor.wait_ge(sem_v.h, at_evac)
                            for c in range(NCH):
                                mm = nc.tensor.transpose(
                                    at_ps[:, c // 4, (c % 4) * 128:(c % 4 + 1) * 128],
                                    acc[:, tt, c * 128:(c + 1) * 128], C.ident_f[:])
                            sem_t.inc(mm)
                            nc.vector.wait_ge(sem_t.h, sem_t.n)
                            nc.vector.tensor_copy(
                                accfm[:, 0:4, i * 128:(i + 1) * 128],
                                at_ps[:, 0].rearrange("p (c q) -> p c q", q=128))
                            at_evac = sem_v.n + 1
                            sem_v.inc(nc.vector.tensor_copy(
                                accfm[:, 4:8, i * 128:(i + 1) * 128],
                                at_ps[:, 1].rearrange("p (c q) -> p c q", q=128)))
                        nc.vector.wait_ge(dxf.h, dxf.n)
                        for c in range(NCH):
                            nc.vector.tensor_add(ftmp[:], accfm[:, c], xmb[:, c])
                            iv = nc.vector.scalar_tensor_tensor(xdb[:, c], ftmp[:], 0.5,
                                                                xdb[:, c], ALU.mult, ALU.add)
                        sem_v.inc(iv)
                        nc.gpsimd.wait_ge(sem_v.h, sem_v.n)
                        dblk = dst[s, :, t0:t0 + TB].rearrange("(c p) t -> p c t", p=128)
                        sxf.inc(nc.gpsimd.dma_start(dblk, xdb[:]), 16)
                        store_chk = sxf.n
    return [sxf]


def host_l1ffn_weights(inp):
    def tolayout(w, width):
        return np.ascontiguousarray(w.reshape(NCH, 128, width).transpose(1, 0, 2)).reshape(128, NCH * width)
    H, DH = 16, 64
    pos = np.arange(T, dtype=np.float64) / 32.0
    inv_freq = 1.0 / (150000.0 ** (np.arange(0, DH, 2, dtype=np.float64) / DH))
    fr = pos[:, None] * inv_freq[None, :]
    sin = np.sin(fr).astype(np.float32).reshape(NTT, 128, 32).transpose(1, 0, 2)
    cos = np.cos(fr).astype(np.float32).reshape(NTT, 128, 32).transpose(1, 0, 2)
    sincos = np.concatenate([sin.reshape(128, NTT * 32), cos.reshape(128, NTT * 32)], axis=1)
    a = np.arange(128)[:, None]
    brel = np.arange(256)[None, :]
    maskb = np.where((brel >= a) & (brel <= a + 128), 0.0, -1e30).astype(np.float32)
    m0 = np.where(np.arange(128)[None, :] <= a, 0.0, -1e30).astype(np.float32)
    wup = np.stack([tolayout(inp["L1_ffn_W_up"][e], 2 * DFF) for e in range(NE)])
    wdn = np.stack([tolayout(inp["L1_ffn_W_down"][e], 1024) for e in range(NE)])
    return {
        "f1_qkvw": tolayout(np.ascontiguousarray(inp["L1_ffn_qkv_w"]), 1536),
        "f1_qkvb": np.ascontiguousarray(np.tile(inp["L1_ffn_qkv_b"], (128, 1))),
        "f1_projw": tolayout(np.ascontiguousarray(inp["L1_ffn_proj_w"]), 1024),
        "f1_projb": np.ascontiguousarray(inp["L1_ffn_proj_b"].reshape(NCH, 128).T),
        "f1_anw": np.ascontiguousarray(inp["L1_ffn_attn_norm_w"].reshape(NCH, 128).T),
        "f1_fnw": np.ascontiguousarray(inp["L1_ffn_ffn_norm_w"].reshape(NCH, 128).T),
        "f1_sink": np.ascontiguousarray(np.tile(inp["L1_ffn_attn_sink"], (128, 1))),
        "f1_gatew": tolayout(np.ascontiguousarray(inp["L1_ffn_gate_w"]), NE),
        "f1_gateb": np.ascontiguousarray(np.tile(inp["L1_ffn_gate_b"], (128, 1))),
        "f1_sincos": np.ascontiguousarray(sincos),
        "f1_mask": maskb,
        "f1_mask0": m0,
        "f1_wup": np.ascontiguousarray(wup),
        "f1_wdn": np.ascontiguousarray(wdn),
    }


def declare_l1ffn_inputs(b: Builder):
    b.din("f1_qkvw", [128, NCH * 1536])
    b.din("f1_qkvb", [128, 1536])
    b.din("f1_projw", [128, NCH * 1024])
    b.din("f1_projb", [128, NCH])
    b.din("f1_anw", [128, NCH])
    b.din("f1_fnw", [128, NCH])
    b.din("f1_sink", [128, 16])
    b.din("f1_gatew", [128, NCH * NE])
    b.din("f1_gateb", [128, NE])
    b.din("f1_sincos", [128, 2 * NTT * 32])
    b.din("f1_mask", [128, 256])
    b.din("f1_mask0", [128, 128])
    b.din("f1_wup", [NE, 128, NCH * 2 * DFF])
    b.din("f1_wdn", [NE, 128, NCH * 1024])
    b.dint("f1_mid", [BL, D, T])


TPAD = 2064
DEBUG_A1 = False


def emit_l1attn(b: Builder, C, src, dst):
    nc = b.nc
    sem_t, sem_v, sem_s = b.sem("m_t"), b.sem("m_v"), b.sem("m_s")
    dxa = b.sem("a1_dx")
    wla = b.sem("a1_wl")
    sxa = b.sem("a1_sx")
    KW = 16

    with b.scope():
        xln = b.sb("a1_xln", [128, NCH, TPAD], BF16)

        for s in range(BL):
            with b.scope():
                ln = LNState(b, "a1ln")
                xdb = b.sb("a1_xdb", [128, NCH, TB], F32)
                tmp = b.sb("a1_tmp", [128, TB], F32)
                phase_gate(b, (nc.sync, nc.vector))
                nc.vector.memset(xln[:, :, 0:7], 0.0)
                sem_v.inc(nc.vector.memset(xln[:, :, 7 + T:TPAD], 0.0))
                v_done = s_done = None
                for tb in range(NTB):
                    if v_done is not None:
                        nc.sync.wait_ge(sem_v.h, v_done)
                        nc.sync.wait_ge(sem_s.h, s_done)
                    blk = src[s, :, tb * TB:(tb + 1) * TB].rearrange("(c p) t -> p c t", p=128)
                    dxa.inc(nc.sync.dma_start(xdb[:], blk), 16)
                    for eng in (nc.vector, nc.scalar, nc.tensor):
                        eng.wait_ge(dxa.h, dxa.n)
                    xch = [xdb[:, c] for c in range(NCH)]
                    emit_ln_stats(b, C, ln, xch, sem_t, sem_v, sem_s)
                    dsts = [xln[:, c, 7 + tb * TB:7 + (tb + 1) * TB] for c in range(NCH)]
                    emit_ln_normalize(b, ln, xch, dsts, tmp, sem_t, sem_v)
                    v_done, s_done = sem_v.n, sem_s.n

            with b.scope():
                proj = b.sb("a1_proj", [128, NCH, TPAD], BF16)
                qf = b.sb("a1_q", [128, NCH, T], BF16)
                vcol = b.sb("a1_vcol", [128, NCH], BF16)
                with b.scope():
                    w1 = b.sb("a1_w1", [128, NCH, 1024], BF16)
                    w2 = b.sb("a1_w2", [128, NCH, 1024], BF16)
                    ps = b.ps("a1_pp", [128, 2, TB], F32)
                    phase_gate(b, (nc.gpsimd,))
                    wla.inc(nc.gpsimd.dma_start(w1[:].rearrange("p c n -> p (c n)"), b.dram["a1_w1"]), 16)
                    wla.inc(nc.gpsimd.dma_start(w2[:].rearrange("p c n -> p (c n)"), b.dram["a1_w2"]), 16)
                    wla.inc(nc.gpsimd.dma_start(vcol[:], b.dram["a1_v"]), 16)
                    nc.tensor.wait_ge(wla.h, wla.n)
                    nc.tensor.wait_ge(sem_v.h, sem_v.n)
                    bank = [Gate(), Gate()]
                    kgrp = 0
                    pblocks = [(i * TB, TB) for i in range(4)] + [(4 * TB, TPAD - 4 * TB)]
                    for wsel, wbuf in ((1, w2), (0, w1)):
                        blocks = pblocks if wsel == 1 else [(i * TB, TB) for i in range(4)]
                        for oc in range(NCH):
                            for t0, tw in blocks:
                                pp = kgrp % 2
                                kgrp += 1
                                bank[pp].wait(nc.tensor)
                                for kc in range(NCH):
                                    src_ap = (xln[:, kc, t0:t0 + tw] if wsel == 1
                                              else xln[:, kc, 7 + t0:7 + t0 + tw])
                                    mm = nc.tensor.matmul(ps[:, pp, 0:tw],
                                                          wbuf[:, kc, oc * 128:(oc + 1) * 128],
                                                          src_ap, start=(kc == 0), stop=(kc == NCH - 1))
                                sem_t.inc(mm)
                                nc.scalar.wait_ge(sem_t.h, sem_t.n)
                                dst_ap = (proj[:, oc, t0:t0 + tw] if wsel == 1
                                          else qf[:, oc, t0:t0 + tw])
                                sem_s.inc(nc.scalar.copy(dst_ap, ps[:, pp, 0:tw]))
                                bank[pp].set(sem_s)

                wT = b.sb("a1_wT", [1, 16, T], BF16)
                with b.scope():
                    ak = b.sb("a1_ak", [128, NCH, T], BF16)
                    akt = b.sb("a1_akt", [128, T], BF16)
                    wtm = b.sb("a1_wtm", [128, 16], BF16)
                    negm = b.sb("a1_negm", [128, 1], F32)
                    den = b.sb("a1_den", [128, 1], F32)
                    rec = b.sb("a1_rec", [128, 1], F32)
                    sc_ps = b.ps("a1_sc", [128, NTT * 16], F32)
                    wt_ps = b.ps("a1_wt", [1, 16, 128], BF16)
                    nc.tensor.wait_ge(sem_s.h, sem_s.n)
                    for k in range(KW):
                        nc.vector.wait_ge(sem_s.h, sem_s.n)
                        nc.vector.wait_ge(sem_t.h, sem_t.n)
                        for c in range(NCH):
                            nc.vector.wait_ge(sem_s.h, sem_s.n)
                            sem_v.inc(nc.vector.tensor_add(akt[:], proj[:, c, k:k + T], qf[:, c]))
                            nc.scalar.wait_ge(sem_v.h, sem_v.n)
                            sem_s.inc(nc.scalar.activation(ak[:, c], akt[:], AF.Tanh))
                        nc.tensor.wait_ge(sem_s.h, sem_s.n)
                        if k == 0:
                            nc.tensor.wait_ge(sem_v.h, sem_v.n)
                        for tt in range(NTT):
                            for kc in range(NCH):
                                mm = nc.tensor.matmul(sc_ps[:, tt * 16 + k:tt * 16 + k + 1],
                                                      ak[:, kc, tt * 128:(tt + 1) * 128],
                                                      vcol[:, kc:kc + 1],
                                                      start=(kc == 0), stop=(kc == NCH - 1))
                        sem_t.inc(mm)
                    if DEBUG_A1 and s == 0:
                        scdump = b.sb("a1_scdump", [128, NTT * 16], F32)
                        nc.vector.wait_ge(sem_t.h, sem_t.n)
                        sem_v.inc(nc.vector.tensor_copy(scdump[:], sc_ps[:]))
                        dbg = b.sem("a1_dbg")
                        nc.gpsimd.wait_ge(sem_v.h, sem_v.n)
                        dbg.inc(nc.gpsimd.dma_start(b.dram["dbg1_sc"], scdump[:]), 16)
                    scv = sc_ps[:].rearrange("p (a b) -> p a b", b=16)
                    for tt in range(NTT):
                        nc.vector.wait_ge(sem_t.h, sem_t.n)
                        vred(b, lambda: nc.vector.reduce_max(negm[:], scv[:, tt], axis=AX.X, negate=True))
                        sem_v.inc(nc.vector.engine_nop())
                        nc.scalar.wait_ge(sem_v.h, sem_v.n)
                        sem_s.inc(nc.scalar.activation(wtm[:], scv[:, tt], AF.Exp,
                                                       bias=negm[:, 0:1], accum_out=den[:, 0:1]))
                        nc.vector.wait_ge(sem_s.h, sem_s.n)
                        vred(b, lambda: nc.vector.reciprocal(rec[:], den[:]))
                        sem_v.inc(nc.vector.tensor_single_scalar(wtm[:], wtm[:], rec[:, 0:1], ALU.mult))
                        nc.tensor.wait_ge(sem_v.h, sem_v.n)
                        for k in range(KW):
                            tr = nc.tensor.transpose(wt_ps[0:1, k], wtm[:, k:k + 1], C.ident_bf[:])
                        sem_t.inc(tr)
                        nc.vector.wait_ge(sem_t.h, sem_t.n)
                        sem_v.inc(nc.vector.tensor_copy(wT[0:1, :, tt * 128:(tt + 1) * 128],
                                                        wt_ps[0:1, :, :]))

                if DEBUG_A1 and s == 0:
                    dbg = b.sem("a1_dbg")
                    nc.gpsimd.wait_ge(b.sem("m_v").h, b.sem("m_v").n)
                    nc.gpsimd.wait_ge(b.sem("m_s").h, b.sem("m_s").n)
                    dbg.inc(nc.gpsimd.dma_start(b.dram["dbg1_proj"], proj[:].rearrange("p c t -> p (c t)")), 16)
                    dbg.inc(nc.gpsimd.dma_start(b.dram["dbg1_q"], qf[:].rearrange("p c t -> p (c t)")), 16)
                    dbg.inc(nc.gpsimd.dma_start(b.dram["dbg1_wt"], wT[0:1].rearrange("p a t -> p (a t)")), 16)
                    nc.gpsimd.wait_ge(dbg.h, dbg.n)

                with b.scope():
                    acc = b.sb("a1_acc", [128, NCH, TB], F32)
                    otmp = b.sb("a1_otmp", [128, TB], F32)
                    xdb = b.sb("a1_xdb2", [128, NCH, TB], F32)
                    wb_ps = b.ps("a1_wb", [128, 2, TB], F32)
                    store_chk = None
                    wbank = [Gate(), Gate()]
                    for tb in range(NTB):
                        if store_chk is not None:
                            nc.sync.wait_ge(sxa.h, store_chk)
                        else:
                            phase_gate(b, (nc.sync,))
                        blk = src[s, :, tb * TB:(tb + 1) * TB].rearrange("(c p) t -> p c t", p=128)
                        dxa.inc(nc.sync.dma_start(xdb[:], blk), 16)
                        for k in range(KW):
                            pp = k % 2
                            wbank[pp].wait(nc.tensor)
                            if tb == 0 and k < 2:
                                nc.tensor.wait_ge(sem_v.h, sem_v.n)
                            mm = nc.tensor.matmul(wb_ps[:, pp], C.ones1b[:],
                                                  wT[0:1, k, tb * TB:(tb + 1) * TB],
                                                  start=True, stop=True)
                            sem_t.inc(mm)
                            nc.vector.wait_ge(sem_t.h, sem_t.n)
                            for c in range(NCH):
                                xsl = xln[:, c, tb * TB + k:tb * TB + k + TB]
                                if k == 0:
                                    iv = nc.vector.tensor_mul(acc[:, c], xsl, wb_ps[:, pp])
                                else:
                                    nc.vector.tensor_mul(otmp[:], xsl, wb_ps[:, pp])
                                    iv = nc.vector.tensor_add(acc[:, c], acc[:, c], otmp[:])
                            sem_v.inc(iv)
                            wbank[pp].set(sem_v)
                        nc.vector.wait_ge(dxa.h, dxa.n)
                        for c in range(NCH):
                            iv = nc.vector.tensor_add(xdb[:, c], xdb[:, c], acc[:, c])
                        sem_v.inc(iv)
                        nc.gpsimd.wait_ge(sem_v.h, sem_v.n)
                        dblk = dst[s, :, tb * TB:(tb + 1) * TB].rearrange("(c p) t -> p c t", p=128)
                        sxa.inc(nc.gpsimd.dma_start(dblk, xdb[:]), 16)
                        store_chk = sxa.n
    return [sxa]


def host_l1attn_weights(inp):
    def tolayout(w):
        return np.ascontiguousarray(w.reshape(NCH, 128, D).transpose(1, 0, 2)).reshape(128, NCH * D)
    return {
        "a1_w1": tolayout(np.ascontiguousarray(inp["L1_attn_w1"])),
        "a1_w2": tolayout(np.ascontiguousarray(inp["L1_attn_w2"])),
        "a1_v": np.ascontiguousarray(inp["L1_attn_v"].reshape(NCH, 128).T),
    }


def declare_l1attn_dbg(b: Builder):
    b.dout("dbg1_sc", [128, NTT * 16], F32)
    b.dout("dbg1_proj", [128, NCH * TPAD], mybir.dt.bfloat16)
    b.dout("dbg1_q", [128, NCH * T], mybir.dt.bfloat16)
    b.dout("dbg1_wt", [1, 16 * T], mybir.dt.bfloat16)


def declare_l1attn_inputs(b: Builder):
    b.din("a1_w1", [128, NCH * D])
    b.din("a1_w2", [128, NCH * D])
    b.din("a1_v", [128, NCH])


def declare_l0attn_dbg(b: Builder):
    b.dout("dbg_q", [128, NCH * T], mybir.dt.bfloat16)
    b.dout("dbg_k", [128, NCH * T], mybir.dt.bfloat16)
    b.dout("dbg_v", [128, NTT * 1024], mybir.dt.bfloat16)
    b.dout("dbg_afm", [128, NCH * T], mybir.dt.bfloat16)


def declare_l0attn_inputs(b: Builder):
    b.din("a0_wq", [128, NCH * D])
    b.din("a0_wk", [128, NCH * D])
    b.din("a0_wv", [128, NCH * D])
    b.din("a0_wo", [128, NCH * D])
    b.din("a0_E2", [128, T])
    b.din("a0_bqk", [128, 2 * NCH])
    b.din("a0_bv", [128, NCH])
    b.din("a0_bo", [128, NCH])


def declare_l0ffn_inputs(b: Builder):
    for n, shp in [("f0_wr", [128, NCH * D]), ("f0_wk", [128, NCH * D]),
                   ("f0_wv", [128, NCH * D]), ("f0_mu4", [128, 4 * NCH])]:
        b.din(n, shp)


def emit_stage_f0(b, C, src, dst):
    with b.scope():
        st = alloc_l0ffn(b)
        emit_l0ffn_weights(b, st)
        return emit_l0ffn(b, C, st, src, dst)


def emit_stage_a0(b, C, src, dst):
    return emit_l0attn(b, C, src, dst)


def emit_stage_a1(b, C, src, dst):
    return emit_l1attn(b, C, src, dst)


def emit_stage_f1(b, C, src, dst):
    return emit_l1ffn(b, C, src, dst)


STAGE_DECL = {"f0": declare_l0ffn_inputs, "a0": declare_l0attn_inputs,
              "a1": declare_l1attn_inputs, "f1": declare_l1ffn_inputs}
STAGE_EMIT = {"f0": emit_stage_f0, "a0": emit_stage_a0, "a1": emit_stage_a1,
              "f1": emit_stage_f1}
STAGE_HOST = {"f0": host_l0ffn_weights, "a0": host_l0attn_weights,
              "a1": host_l1attn_weights, "f1": host_l1ffn_weights}

FCONSTS = [EPS_LN]


def emit_barrier(b):
    nc = b.nc
    for eng in (nc.sync, nc.gpsimd, nc.vector, nc.scalar, nc.tensor):
        for s in b.sems.values():
            if s.n:
                eng.wait_ge(s.h, s.n)


def build_stages(stages):
    ctx = contextlib.ExitStack()
    nc = bass.Bass("TRN2", target_bir_lowering=False, detect_race_conditions=RACE_DETECT)
    b = Builder(nc, ctx)
    xin = b.din("xin", [BL, D, T])
    xout = b.dout("xout", [BL, D, T])
    xs = b.dint("xs", [BL, D, T]) if len(stages) > 1 else None
    for stg in dict.fromkeys(stages):
        STAGE_DECL[stg](b)
    if DEBUG_A0 and "a0" in stages:
        declare_l0attn_dbg(b)
    if DEBUG_A1 and "a1" in stages:
        declare_l1attn_dbg(b)
    if DEBUG_F1 and "f1" in stages:
        b.dout("dbgf_h", [128, NCH * T], mybir.dt.bfloat16)
        b.dout("dbgf_q", [128, NCH * T], mybir.dt.bfloat16)
        b.dout("dbgf_k", [128, 4 * T], mybir.dt.bfloat16)
        b.dout("dbgf_v", [128, NTT * 256], mybir.dt.bfloat16)
        b.dout("dbgf_y", [128, NCH * T], mybir.dt.bfloat16)
        b.dout("dbgf_h2", [128, NCH * T], mybir.dt.bfloat16)
        b.dout("dbgf_ce", [128, NTT * NE], F32)
        b.dout("dbgf_msk", [128, 256], F32)
        b.dout("dbgf_g0", [128, NE], F32)
        b.dout("dbgf_g1", [128, NE], F32)
        b.dout("dbgf_eb", [128, NE], F32)
        b.dout("dbgf_ge", [128, NE], F32)
        b.dout("dbgf_gcols", [128, 8], F32)
        b.dout("dbgf_P", [128, 256], mybir.dt.bfloat16)
        b.dout("dbgf_cols", [128, 8], F32)
        b.dout("dbgf_acc", [128, NTT * 1024], F32)
    with ctx:
        C = Consts(b)
        sem_v = b.sem("m_v")
        b.init_fconsts(FCONSTS, sem_v)
        cs1, cs2 = C.emit_loads(b)
        for eng in (nc.vector, nc.scalar, nc.tensor, nc.gpsimd, nc.sync):
            eng.wait_ge(cs1.h, cs1.n)
            eng.wait_ge(cs2.h, cs2.n)
            eng.wait_ge(sem_v.h, sem_v.n)
        stores = []
        for i, stg in enumerate(stages):
            src = xin if i == 0 else xs
            dst = xout if i == len(stages) - 1 else xs
            if i > 0:
                emit_barrier(b)
            stores = STAGE_EMIT[stg](b, C, src, dst)
        for s in stores:
            nc.sync.wait_ge(s.h, s.n)
    return b


STAGES = ["f0", "a0", "f0", "f1", "a1", "f1"]


def kernel(**inputs):
    x = inputs["L0_attn_x"]
    xT = np.ascontiguousarray(x.transpose(0, 2, 1))
    shards = xT.reshape(NCORES, BL, D, T)
    m0 = host_consts(inputs)
    for stg in dict.fromkeys(STAGES):
        m0.update(STAGE_HOST[stg](inputs))
    b = build_stages(STAGES)
    in_maps = []
    for ci in range(NCORES):
        m = dict(m0)
        m["xin"] = shards[ci]
        in_maps.append(m)
    res = run_bass_kernel_spmd(b.nc, in_maps, core_ids=list(range(NCORES)))
    out = np.stack([res.results[i]["xout"] for i in range(NCORES)])
    out = out.reshape(B, D, T).transpose(0, 2, 1)
    return np.ascontiguousarray(out)
